# version 07 (final)
# baseline (speedup 1.0000x reference)
import numpy as np
import ml_dtypes
from contextlib import ExitStack

import concourse.bass as bass
import concourse.mybir as mybir
from concourse.bass_utils import run_bass_kernel_spmd

F32 = mybir.dt.float32
BF16 = mybir.dt.bfloat16
AF = mybir.ActivationFunctionType
ALU = mybir.AluOpType
AX = mybir.AxisListType

D = 2048
H = 16
DH = 128
DL = 64
E = 16
EI = 1408
NI = EI // 128
I2 = 2816
NEX = 18
NCORES = 8
BIG_NEG = -1.0e30
ATTN_B_SCALE = float(1.0 / np.sqrt(np.sqrt(128.0)))

bfdt = ml_dtypes.bfloat16


def rope_tables(T, d, scale=1.0):
    inv = (1.0 / 10000.0 ** (np.arange(0, d, 2, dtype=np.float32) / np.float32(d))).astype(np.float32)
    ang = np.arange(T, dtype=np.float32)[:, None] * inv[None, :]
    c = np.cos(ang).astype(np.float32)
    s = np.sin(ang).astype(np.float32)
    C = np.empty((T, d), np.float32)
    S = np.empty((T, d), np.float32)
    C[:, 0::2] = c
    C[:, 1::2] = c
    S[:, 0::2] = -s
    S[:, 1::2] = s
    return (C * scale).astype(np.float32), (S * scale).astype(np.float32)


class Sem:
    def __init__(self, prog, name):
        self.name = name
        self.h = prog.ctx.enter_context(prog.nc.semaphore(name))
        self.n = 0

    def inc(self, inst, k=1):
        inst.then_inc(self.h, k)
        self.n += k
        return self.n

    def dma(self, eng, out, in_):
        return self.inc(eng.dma_start(out=out, in_=in_), 16)


class Prog:

    def __init__(self, T, table=None):
        self.T = T
        self.NTS = T // 128
        self.TOK = 2 * T
        self.NT = self.TOK // 128
        self.table = {} if table is None else table
        self.recording = table is None
        self.nc = bass.Bass("TRN2", target_bir_lowering=False)
        self.ctx = ExitStack()
        self.sems = {}
        self.out_sems = []
        self.missing = set()
        self._declare_dram()
        self._alloc_global()

    def evt(self, key, sem):
        if self.recording:
            self.table[key] = (sem.name, sem.n)

    def wait(self, eng, key):
        if self.recording or key not in self.table:
            if key not in self.table:
                self.missing.add(key)
            eng.wait_ge(self._dummy.h, 0)
        else:
            name, n = self.table[key]
            eng.wait_ge(self.sem(name).h, n)

    def sem(self, name):
        if name not in self.sems:
            self.sems[name] = Sem(self, name)
        return self.sems[name]

    def _declare_dram(self):
        nc, T, TOK = self.nc, self.T, self.TOK
        dt = nc.dram_tensor
        ap = lambda t: t.ap()
        self.x_in = ap(dt("x_in", [TOK, D], F32, kind="ExternalInput"))
        self.out = ap(dt("out", [TOK, D], F32, kind="ExternalOutput"))
        self.wq = ap(dt("wq", [D, D], F32, kind="ExternalInput"))
        self.wk_lat = ap(dt("wk_lat", [D, DL], F32, kind="ExternalInput"))
        self.wv_lat = ap(dt("wv_lat", [D, DL], F32, kind="ExternalInput"))
        self.kexp = ap(dt("kexp", [H, DL, DH], F32, kind="ExternalInput"))
        self.vexp = ap(dt("vexp", [H, DL, DH], F32, kind="ExternalInput"))
        self.projw = ap(dt("projw", [D, D], F32, kind="ExternalInput"))
        for n in ("wqt", "wkt", "wvt", "wot"):
            setattr(self, n, ap(dt(n, [D, D], F32, kind="ExternalInput")))
        for n in ("bq", "bk", "bv", "bo"):
            setattr(self, n, ap(dt(n, [D], F32, kind="ExternalInput")))
        self.wgt = ap(dt("wgt", [D, E], F32, kind="ExternalInput"))
        self.w1t = ap(dt("w1t", [E, D, EI], F32, kind="ExternalInput"))
        self.w3t = ap(dt("w3t", [E, D, EI], F32, kind="ExternalInput"))
        self.w2t = ap(dt("w2t", [E, EI, D], F32, kind="ExternalInput"))
        self.ws1t = ap(dt("ws1t", [D, I2], F32, kind="ExternalInput"))
        self.ws3t = ap(dt("ws3t", [D, I2], F32, kind="ExternalInput"))
        self.ws2t = ap(dt("ws2t", [I2, D], F32, kind="ExternalInput"))
        self.eye = ap(dt("eye", [128, 128], BF16, kind="ExternalInput"))
        self.tri = ap(dt("tri", [128, 128], F32, kind="ExternalInput"))
        self.trit = ap(dt("trit", [128, 2048], BF16, kind="ExternalInput"))
        self.b2_scr = ap(dt("b2_scr", [2, T], BF16))
        self.cqr = ap(dt("cqr", [T, D], F32, kind="ExternalInput"))
        self.sqr = ap(dt("sqr", [T, D], F32, kind="ExternalInput"))
        self.ckr = ap(dt("ckr", [T, DL], F32, kind="ExternalInput"))
        self.skr = ap(dt("skr", [T, DL], F32, kind="ExternalInput"))
        self.q_scr = ap(dt("q_scr", [2, H, DH, T], BF16))
        self.x1_scr = ap(dt("x1_scr", [TOK, D], F32))
        self.x2_scr = ap(dt("x2_scr", [TOK, D], F32))
        self.x3_scr = ap(dt("x3_scr", [TOK, D], F32))
        self.xnt_scr = ap(dt("xnt_scr", [self.NT, 128, D], BF16))
        self.qt_scr = ap(dt("qt_scr", [2, H, DH, T], BF16))
        self.kt_scr = ap(dt("kt_scr", [2, H, DH, T], BF16))
        self.v_scr = ap(dt("v_scr", [2, H, 128, T // 128, 128], BF16))
        self.wc1 = ap(dt("wc1", [NEX, NI, 128, 16, 128], BF16))
        self.wc3 = ap(dt("wc3", [NEX, NI, 128, 16, 128], BF16))
        self.wc2 = ap(dt("wc2", [NEX, 2, 128, NI, 1024], BF16))

    def _alloc_global(self):
        nc, c, T = self.nc, self.ctx, self.T
        sb = lambda n, sh, d: c.enter_context(nc.sbuf_tensor(n, sh, d))
        self.eye_b = sb("eye_b", [128, 128], BF16)
        self.eye_f = sb("eye_f", [128, 128], F32)
        self.tri_f = sb("tri_f", [128, 128], F32)
        self.trit_b = sb("trit_b", [128, 2048], BF16)
        self.ones_b = sb("ones_b", [1, 128], BF16)
        self.ones2_b = sb("ones2_b", [2, 128], BF16)
        self.stat = sb("stat", [128, 32], F32)
        self.sqt = sb("sqt", [128, D // 2], F32)
        self.PA = c.enter_context(nc.psum_tensor("PA", [128, 2048], F32))
        self.PB = c.enter_context(nc.psum_tensor("PB", [128, 1024], F32))
        self.PC = c.enter_context(nc.psum_tensor("PC", [128, 1024], F32))
        self._dummy = Sem(self, "dummy")

    def emit_all(self):
        self.p_init()
        self.p_convert_ffn_weights()
        self.p_attn_a()
        self.p_ffn(self.x1_scr, self.x2_scr, tag="f1")
        self.p_attn_b()
        self.p_ffn(self.x3_scr, self.out, tag="f2")
        self.p_finish()
        self.ctx.close()
        return self.nc

    def p_init(self):
        nc = self.nc
        sI = self.sem("d_init")
        vB = self.sem("v_init")
        with nc.Block(no_gpsimd_drain=True) as b:
            @b.sync
            def _(e):
                sI.dma(e, self.eye_b[:, :], self.eye)
                sI.dma(e, self.tri_f[:, :], self.tri)
                sI.dma(e, self.trit_b[:, :], self.trit)
            @b.vector
            def _(e):
                e.memset(self.ones_b[:, :], 1.0)
                e.memset(self.ones2_b[:, :], 1.0)
                vB.inc(e.memset(self.stat[:, 16:17], 1e-5))
                e.wait_ge(sI.h, 16 * 3)
                vB.inc(e.tensor_copy(self.eye_f[:, :], self.eye_b[:, :]))
                self.evt("v:init", vB)

    def p_convert_ffn_weights(self):
        nc = self.nc
        sL = [self.sem(f"cw_l{i}") for i in range(2)]
        sSg = [self.sem(f"cw_s{i}") for i in range(2)]
        vC = self.sem("cw_v")
        c = ExitStack()
        stg = [c.enter_context(nc.sbuf_tensor(f"cw_stg{i}", [128, 2048], F32)) for i in range(2)]
        obf = [c.enter_context(nc.sbuf_tensor(f"cw_obf{i}", [128, 2048], BF16)) for i in range(2)]

        jobs = []
        for ex in range(NEX):
            if ex < E:
                w1, w3, w2 = self.w1t[ex], self.w3t[ex], self.w2t[ex]
            else:
                sh = ex - E
                w1 = self.ws1t[:, sh * EI:(sh + 1) * EI]
                w3 = self.ws3t[:, sh * EI:(sh + 1) * EI]
                w2 = self.ws2t[sh * EI:(sh + 1) * EI, :]
            for cch in range(16):
                jobs.append((w1[cch * 128:(cch + 1) * 128, :], EI,
                             [(self.wc1[ex, i, :, cch, :], i * 128, 128) for i in range(NI)]))
                jobs.append((w3[cch * 128:(cch + 1) * 128, :], EI,
                             [(self.wc3[ex, i, :, cch, :], i * 128, 128) for i in range(NI)]))
            for i in range(NI):
                jobs.append((w2[i * 128:(i + 1) * 128, :], D,
                             [(self.wc2[ex, hf, :, i, :], hf * 1024, 1024) for hf in range(2)]))

        with nc.Block(no_gpsimd_drain=True) as b:
            @b.sync
            def _(e):
                for j, (src, w, stores) in enumerate(jobs):
                    if j >= 2:
                        self.wait(e, f"cw:v:{j - 2}")
                    sL[j % 2].dma(e, stg[j % 2][:, 0:w], src)
            @b.vector
            def _(e):
                for j, (src, w, stores) in enumerate(jobs):
                    e.wait_ge(sL[j % 2].h, 16 * (j // 2 + 1))
                    if j >= 2:
                        self.wait(e, f"cw:g:{j - 2}")
                    vC.inc(e.tensor_copy(obf[j % 2][:, 0:w], stg[j % 2][:, 0:w]))
                    self.evt(f"cw:v:{j}", vC)
            @b.gpsimd
            def _(e):
                for j, (src, w, stores) in enumerate(jobs):
                    self.wait(e, f"cw:v:{j}")
                    for (dst, c0, cw) in stores:
                        sSg[j % 2].dma(e, dst, obf[j % 2][:, c0:c0 + cw])
                    self.evt(f"cw:g:{j}", sSg[j % 2])
        self.evt("cw:done0", sSg[0])
        self.evt("cw:done1", sSg[1])
        self.n_cw_jobs = len(jobs)
        c.close()

    def p_attn_a(self):
        nc, T, NTS, NT = self.nc, self.T, self.NTS, self.NT
        PA, PB, PC = self.PA, self.PB, self.PC
        cA = ExitStack()
        sbA = lambda n, sh, d: cA.enter_context(nc.sbuf_tensor(n, sh, d))
        self.klatT = sbA("klatT", [64, 2 * T], BF16)
        self.vlatT = sbA("vlatT", [64, 2 * T], BF16)
        self.kexp_b = sbA("kexp_b", [64, H * 128], BF16)
        self.vexp_b = sbA("vexp_b", [64, H * 128], BF16)
        wstg = [sbA(f"a0_ws{i}", [128, D], F32) for i in range(2)]

        cP0 = ExitStack()
        sb = lambda n, sh, d: cP0.enter_context(nc.sbuf_tensor(n, sh, d))
        x_f = [sb(f"a0_x{i}", [128, D], F32) for i in range(2)]
        tabc = [sb(f"a0_tc{i}", [128, DH], F32) for i in range(2)]
        tabs = [sb(f"a0_ts{i}", [128, DH], F32) for i in range(2)]
        tabk = [sb(f"a0_tk{i}", [128, 2 * DL], F32) for i in range(2)]
        xn_b = [sb(f"a0_xn{i}", [128, D], BF16) for i in range(2)]
        xnt_b = [sb(f"a0_xnt{i}", [128, D], BF16) for i in range(2)]
        u_f = sb("a0_uf", [128, D], F32)
        t1_f = sb("a0_t1", [128, D], F32)
        qr_b = sb("a0_qr", [128, D], BF16)
        qst_b = [sb(f"a0_qs{i}", [128, D], BF16) for i in range(2)]
        lat_f = sb("a0_lat", [128, 2 * DL], F32)
        lu_f = sb("a0_lu", [128, DL], F32)
        lt1_f = sb("a0_lt1", [128, DL], F32)
        kr_b = sb("a0_kr", [128, DL], BF16)
        vr_b = sb("a0_vr", [128, DL], BF16)
        wq_b = sb("a0_wqb", [128, 16 * D], BF16)
        wkv_b = sb("a0_wkv", [128, 16 * 2 * DL], BF16)

        dX = [self.sem(f"a0_dx{i}") for i in range(2)]
        dT_ = [self.sem(f"a0_dt{i}") for i in range(2)]
        dW = [self.sem(f"a0_dw{i}") for i in range(2)]
        dQS = [self.sem(f"a0_dqs{i}") for i in range(2)]
        tM = self.sem("a0_t")
        vV = self.sem("a0_v")
        sS = self.sem("a0_s")
        ncw = self.n_cw_jobs

        with nc.Block(no_gpsimd_drain=True) as b:
            @b.sync
            def _(e):
                for j in range(16):
                    if j == 0:
                        self.wait(e, f"cw:v:{ncw - 1}")
                        self.wait(e, "cw:done0")
                        self.wait(e, "cw:done1")
                    if j >= 2:
                        self.wait(e, f"a0:wcv:{j - 2}")
                    dW[j % 2].dma(e, wstg[j % 2][:, :], self.wq[j * 128:(j + 1) * 128, :])
                for j in range(16):
                    self.wait(e, f"a0:wcv:{14 + j}")
                    dW[j % 2].dma(e, wstg[j % 2][:, 0:DL], self.wk_lat[j * 128:(j + 1) * 128, :])
                    dW[j % 2].dma(e, wstg[j % 2][:, DL:2 * DL], self.wv_lat[j * 128:(j + 1) * 128, :])
                self.wait(e, "a0:wcv:30")
                dW[0].dma(e, wstg[0][0:64, :], self.kexp.rearrange("h d m -> d h m"))
                self.wait(e, "a0:wcv:31")
                dW[1].dma(e, wstg[1][0:64, :], self.vexp.rearrange("h d m -> d h m"))
            @b.vector
            def _(e):
                for j in range(16):
                    if j == 0:
                        self.wait(e, "cw:done0")
                        self.wait(e, "cw:done1")
                    e.wait_ge(dW[j % 2].h, 16 * (j // 2 + 1))
                    vV.inc(e.tensor_copy(wq_b[:, j * D:(j + 1) * D], wstg[j % 2][:, :]))
                    self.evt(f"a0:wcv:{j}", vV)
                for j in range(16):
                    e.wait_ge(dW[j % 2].h, 16 * 8 + 32 * (j // 2 + 1))
                    vV.inc(e.tensor_copy(wkv_b[:, j * 2 * DL:(j + 1) * 2 * DL], wstg[j % 2][:, 0:2 * DL]))
                    self.evt(f"a0:wcv:{16 + j}", vV)
                e.wait_ge(dW[0].h, 16 * 8 + 32 * 8 + 16)
                vV.inc(e.tensor_copy(self.kexp_b[:, :], wstg[0][0:64, :]))
                self.evt("a0:wcv:48", vV)
                e.wait_ge(dW[1].h, 16 * 8 + 32 * 8 + 16)
                vV.inc(e.tensor_copy(self.vexp_b[:, :], wstg[1][0:64, :]))
                self.evt("a0:wcv:49", vV)

        for i in range(NT):
            s, tr = divmod(i, NTS)
            t0 = tr * 128
            with nc.Block(no_gpsimd_drain=True) as b:
                @b.sync
                def _(e):
                    if i >= 2:
                        self.wait(e, f"a0:ln:{i - 2}")
                        self.wait(e, f"a0:rope:{i - 2}")
                    dX[i % 2].dma(e, x_f[i % 2][:, :], self.x_in[i * 128:(i + 1) * 128, :])
                    dT_[i % 2].dma(e, tabc[i % 2][:, :], self.cqr[t0:t0 + 128, 0:DH])
                    dT_[i % 2].dma(e, tabs[i % 2][:, :], self.sqr[t0:t0 + 128, 0:DH])
                    dT_[i % 2].dma(e, tabk[i % 2][:, 0:DL], self.ckr[t0:t0 + 128, :])
                    dT_[i % 2].dma(e, tabk[i % 2][:, DL:2 * DL], self.skr[t0:t0 + 128, :])

                @b.vector
                def _(e):
                    e.wait_ge(dX[i % 2].h, 16 * (i // 2 + 1))
                    vV.inc(e.tensor_reduce(self.stat[:, 0:1], x_f[i % 2][:, :], AX.X, op=ALU.add))
                    self.evt(f"a0:mean0:{i}", vV)
                @b.scalar
                def _(e):
                    e.wait_ge(dX[i % 2].h, 16 * (i // 2 + 1))
                    if i >= 1:
                        self.wait(e, f"a0:var:{i - 1}")
                    e.activation(self.sqt[:, :], x_f[i % 2][:, 0:D // 2], AF.Square,
                                 accum_out=self.stat[:, 1:2])
                    e.drain()
                    sS.inc(e.activation(self.sqt[:, :], x_f[i % 2][:, D // 2:D], AF.Square,
                                        accum_out=self.stat[:, 18:19]))
                    self.evt(f"a0:sumsq:{i}", sS)
                @b.vector
                def _(e):
                    self.wait(e, f"a0:sumsq:{i}")
                    e.drain()
                    e.tensor_tensor(self.stat[:, 1:2], self.stat[:, 1:2], self.stat[:, 18:19], op=ALU.add)
                    e.tensor_scalar(self.stat[:, 2:3], self.stat[:, 0:1], 1.0 / D, None, op0=ALU.mult)
                    e.drain()
                    e.tensor_tensor(self.stat[:, 3:4], self.stat[:, 2:3], self.stat[:, 2:3], op=ALU.mult)
                    e.drain()
                    vV.inc(e.scalar_tensor_tensor(self.stat[:, 4:5], self.stat[:, 1:2], 1.0 / D,
                                                  self.stat[:, 3:4], op0=ALU.mult, op1=ALU.subtract))
                    self.evt(f"a0:var:{i}", vV)
                @b.scalar
                def _(e):
                    self.wait(e, f"a0:var:{i}")
                    sS.inc(e.activation(self.stat[:, 5:6], self.stat[:, 4:5], AF.Sqrt, bias=self.stat[:, 16:17]))
                    self.evt(f"a0:std:{i}", sS)
                @b.vector
                def _(e):
                    self.wait(e, f"a0:std:{i}")
                    e.drain()
                    e.reciprocal(self.stat[:, 6:7], self.stat[:, 5:6])
                    e.drain()
                    if i >= 2:
                        self.wait(e, f"a0:xntr:{i - 2}")
                    vV.inc(e.tensor_scalar(xn_b[i % 2][:, :], x_f[i % 2][:, :], self.stat[:, 2:3],
                                           self.stat[:, 6:7], op0=ALU.subtract, op1=ALU.mult))
                    self.evt(f"a0:ln:{i}", vV)

                @b.tensor
                def _(e):
                    self.wait(e, f"a0:ln:{i}")
                    if i >= 1:
                        self.wait(e, f"a0:qst:{i - 1}")
                    pbb = PB[:, :].bitcast(BF16)
                    for cch in range(16):
                        inst = e.transpose(pbb[:, cch * 128:(cch + 1) * 128],
                                           xn_b[i % 2][:, cch * 128:(cch + 1) * 128], self.eye_b[:, :])
                    tM.inc(inst)
                    self.evt(f"a0:xntr:{i}", tM)
                @b.vector
                def _(e):
                    self.wait(e, f"a0:xntr:{i}")
                    if i >= 2:
                        self.wait(e, f"a0:qmm:{i - 2}")
                    vV.inc(e.tensor_copy(xnt_b[i % 2][:, :], PB[:, :].bitcast(BF16)))
                    self.evt(f"a0:xntcp:{i}", vV)

                @b.tensor
                def _(e):
                    self.wait(e, f"a0:xntcp:{i}")
                    if i == 0:
                        self.wait(e, "a0:wcv:15")
                        self.wait(e, "a0:wcv:31")
                    if i >= 1:
                        self.wait(e, f"a0:qev:{i - 1}")
                        self.wait(e, f"a0:latst:{i - 1}")
                    for nq in range(4):
                        for cch in range(16):
                            e.matmul(PA[:, nq * 512:(nq + 1) * 512],
                                     xnt_b[i % 2][:, cch * 128:(cch + 1) * 128],
                                     wq_b[:, cch * D + nq * 512:cch * D + (nq + 1) * 512],
                                     start=(cch == 0), stop=(cch == 15))
                    for cch in range(16):
                        e.matmul(PC[:, 0:DL], xnt_b[i % 2][:, cch * 128:(cch + 1) * 128],
                                 wkv_b[:, cch * 2 * DL:cch * 2 * DL + DL],
                                 start=(cch == 0), stop=(cch == 15))
                    for cch in range(16):
                        inst = e.matmul(PC[:, DL:2 * DL], xnt_b[i % 2][:, cch * 128:(cch + 1) * 128],
                                        wkv_b[:, cch * 2 * DL + DL:(cch + 1) * 2 * DL],
                                        start=(cch == 0), stop=(cch == 15))
                    tM.inc(inst)
                    self.evt(f"a0:qmm:{i}", tM)

                @b.vector
                def _(e):
                    self.wait(e, f"a0:qmm:{i}")
                    e.wait_ge(dT_[i % 2].h, 16 * 4 * (i // 2 + 1))
                    e.tensor_copy(u_f[:, 0:D:2], PA[:, 1:D:2])
                    e.tensor_copy(u_f[:, 1:D:2], PA[:, 0:D:2])
                    for h in range(H):
                        inst = e.tensor_tensor(t1_f[:, h * 128:(h + 1) * 128], PA[:, h * 128:(h + 1) * 128],
                                               tabc[i % 2][:, :], op=ALU.mult)
                    vV.inc(inst)
                    self.evt(f"a0:qev:{i}", vV)
                    e.drain()
                    for h in range(H):
                        e.tensor_tensor(u_f[:, h * 128:(h + 1) * 128], u_f[:, h * 128:(h + 1) * 128],
                                        tabs[i % 2][:, :], op=ALU.mult)
                    e.drain()
                    if i >= 1:
                        self.wait(e, f"a0:qtr:{i - 1}")
                    vV.inc(e.tensor_tensor(qr_b[:, :], t1_f[:, :], u_f[:, :], op=ALU.add))
                    self.evt(f"a0:qrope:{i}", vV)

                    vV.inc(e.tensor_copy(lat_f[:, :], PC[:, 0:2 * DL]))
                    self.evt(f"a0:latev:{i}", vV)
                    e.drain()
                    e.tensor_copy(lu_f[:, 0:DL:2], lat_f[:, 1:DL:2])
                    e.tensor_copy(lu_f[:, 1:DL:2], lat_f[:, 0:DL:2])
                    e.tensor_tensor(lt1_f[:, :], lat_f[:, 0:DL], tabk[i % 2][:, 0:DL], op=ALU.mult)
                    e.drain()
                    e.tensor_tensor(lu_f[:, :], lu_f[:, :], tabk[i % 2][:, DL:2 * DL], op=ALU.mult)
                    e.drain()
                    if i >= 1:
                        self.wait(e, f"a0:lattr:{i - 1}")
                    e.tensor_tensor(kr_b[:, :], lt1_f[:, :], lu_f[:, :], op=ALU.add)
                    vV.inc(e.tensor_copy(vr_b[:, :], lat_f[:, DL:2 * DL]))
                    self.evt(f"a0:rope:{i}", vV)

                @b.tensor
                def _(e):
                    self.wait(e, f"a0:qrope:{i}")
                    self.wait(e, f"a0:xntcp:{i}")
                    pbb = PB[:, :].bitcast(BF16)
                    for h in range(H):
                        inst = e.transpose(pbb[:, h * 128:(h + 1) * 128],
                                           qr_b[:, h * 128:(h + 1) * 128], self.eye_b[:, :])
                    tM.inc(inst)
                    self.evt(f"a0:qtr:{i}", tM)
                    self.wait(e, f"a0:rope:{i}")
                    pcb = PC[:, 256:512].bitcast(BF16)
                    e.transpose(pcb[0:64, 0:128], kr_b[:, :], self.eye_b[:, :])
                    inst = e.transpose(pcb[0:64, 128:256], vr_b[:, :], self.eye_b[:, :])
                    tM.inc(inst)
                    self.evt(f"a0:lattr:{i}", tM)
                @b.vector
                def _(e):
                    self.wait(e, f"a0:qtr:{i}")
                    if i >= 2:
                        e.wait_ge(dQS[i % 2].h, 16 * 16 * (i // 2))
                    vV.inc(e.tensor_copy(qst_b[i % 2][:, :], PB[:, :].bitcast(BF16)))
                    self.evt(f"a0:qst:{i}", vV)
                    self.wait(e, f"a0:lattr:{i}")
                    pcb = PC[:, 256:512].bitcast(BF16)
                    e.tensor_copy(self.klatT[:, s * T + t0:s * T + t0 + 128], pcb[0:64, 0:128])
                    vV.inc(e.tensor_copy(self.vlatT[:, s * T + t0:s * T + t0 + 128], pcb[0:64, 128:256]))
                    self.evt(f"a0:latst:{i}", vV)
                @b.gpsimd
                def _(e):
                    self.wait(e, f"a0:qst:{i}")
                    for h in range(H):
                        dQS[i % 2].dma(e, self.q_scr[s, h, :, t0:t0 + 128],
                                       qst_b[i % 2][:, h * 128:(h + 1) * 128])
            if tr == NTS - 1:
                self.evt(f"a0:qsA:{s}", dQS[0])
                self.evt(f"a0:qsB:{s}", dQS[1])

        self.evt("a0:done_t", tM)
        self.evt("a0:done_v", vV)
        cP0.close()

        cP1 = ExitStack()
        sb = lambda n, sh, d: cP1.enter_context(nc.sbuf_tensor(n, sh, d))
        qtl = [sb(f"a1_q{i}", [128, T], BF16) for i in range(2)]
        kth = sb("a1_kth", [128, T], BF16)
        vh = sb("a1_vh", [128, T], BF16)
        praw = sb("a1_praw", [128, T], BF16)
        pT = [sb(f"a1_pt{i}", [128, 512], BF16) for i in range(2)]
        bst = {"bstage": sb("a1_bstage", [128, 16], F32),
               "btmp": sb("a1_btmp", [16, 128], F32),
               "bhi": sb("a1_bhi", [16, 128], BF16),
               "blo": sb("a1_blo", [16, 128], BF16),
               "b2row": sb("a1_b2row", [2, T], BF16),
               "sc0": sb("a1_sc0", [128, 512], F32),
               "sc1": sb("a1_sc1", [128, 512], F32)}
        dsb = sb("a1_dsb", [128, 128], F32)
        ot_sb = sb("a1_ot", [128, H * T], BF16)
        pw_b = sb("a1_pw", [128, 16 * 1024], BF16)
        xr_f = sb("a1_xr", [128, D], F32)
        x1_f = sb("a1_x1", [128, D], F32)

        dQT = [self.sem(f"a1_dqt{i}") for i in range(2)]
        dPW = [self.sem(f"a1_dpw{i}") for i in range(2)]
        dXR = self.sem("a1_dxr")
        dX1 = self.sem("a1_dx1")
        self._hist = {}

        for s in range(2):
            for h in range(H):
                u = s * H + h
                self._attn_unit(tag="a1", s=s, h=h, u=u, qsrc=self.q_scr, qtl=qtl,
                                kth=kth, vh=vh, praw=praw, pT=pT, bst=bst, dsb=dsb,
                                ot_sb=ot_sb, dQT=dQT, tM=tM, vV=vV, sS=sS,
                                expand_kv=True, ksrc=None, vsrc=None, dKV=None,
                                prev_phase_done=("a2:x1ev:0:1:%d" % (NTS - 1)) if s == 1 else None,
                                qs_evts=("a0:qsA:1", "a0:qsB:1"),
                                phase_start_evts=("a0:done_t", "a0:done_v") if u == 0 else None)

            for half in range(2):
                hf0 = half * 1024
                with nc.Block(no_gpsimd_drain=True) as b:
                    @b.sync
                    def _(e):
                        for j in range(16):
                            jj = (s * 2 + half) * 16 + j
                            if jj >= 2:
                                pj = jj - 2
                                self.wait(e, f"a1:pwcv:{pj // 32}:{(pj % 32) // 16}:{pj % 16}")
                            else:
                                self.wait(e, f"a0:wcv:{30 + jj}")
                            dPW[j % 2].dma(e, wstg[j % 2][:, 0:1024],
                                           self.projw[j * 128:(j + 1) * 128, hf0:hf0 + 1024])
                    @b.vector
                    def _(e):
                        for j in range(16):
                            jj = (s * 2 + half) * 16 + j
                            if jj == 0:
                                self.wait(e, "a0:done_t")
                            e.wait_ge(dPW[j % 2].h, 16 * (jj // 2 + 1))
                            vV.inc(e.tensor_copy(pw_b[:, j * 1024:(j + 1) * 1024], wstg[j % 2][:, 0:1024]))
                            self.evt(f"a1:pwcv:{s}:{half}:{j}", vV)
                for qi in range(NTS):
                    i = s * NTS + qi
                    t0 = qi * 128
                    with nc.Block(no_gpsimd_drain=True) as b:
                        @b.gpsimd
                        def _(e):
                            if (half, qi) != (0, 0):
                                ph, pq = (half, qi - 1) if qi > 0 else (0, NTS - 1)
                                self.wait(e, f"a2:x1ev:{s}:{ph}:{pq}")
                            dXR.dma(e, xr_f[:, hf0:hf0 + 1024],
                                    self.x_in[i * 128:(i + 1) * 128, hf0:hf0 + 1024])
                        @b.tensor
                        def _(e):
                            if qi == 0:
                                self.wait(e, f"a1:pwcv:{s}:{half}:15")
                                if half == 0:
                                    self.wait(e, f"a1:otdone:{s}")
                            if (half, qi) != (0, 0):
                                ph, pq = (half, qi - 1) if qi > 0 else (0, NTS - 1)
                                self.wait(e, f"a2:x1ev:{s}:{ph}:{pq}")
                            elif s == 1:
                                self.wait(e, "a2:x1ev:0:1:%d" % (NTS - 1))
                            for nq in range(2):
                                for hh in range(H):
                                    inst = e.matmul(PA[:, nq * 512:(nq + 1) * 512],
                                                    ot_sb[:, hh * T + t0:hh * T + t0 + 128],
                                                    pw_b[:, hh * 1024 + nq * 512:hh * 1024 + (nq + 1) * 512],
                                                    start=(hh == 0), stop=(hh == H - 1))
                            tM.inc(inst)
                            self.evt(f"a2:proj:{s}:{half}:{qi}", tM)
                        @b.vector
                        def _(e):
                            self.wait(e, f"a2:proj:{s}:{half}:{qi}")
                            e.wait_ge(dXR.h, 16 * (2 * s * NTS + half * NTS + qi + 1))
                            kk = 2 * s * NTS + half * NTS + qi
                            if kk >= 1:
                                e.wait_ge(dX1.h, 16 * kk)
                            vV.inc(e.tensor_tensor(x1_f[:, hf0:hf0 + 1024], PA[:, 0:1024],
                                                   xr_f[:, hf0:hf0 + 1024], op=ALU.add))
                            self.evt(f"a2:x1ev:{s}:{half}:{qi}", vV)
                        @b.gpsimd
                        def _(e):
                            self.wait(e, f"a2:x1ev:{s}:{half}:{qi}")
                            dX1.dma(e, self.x1_scr[i * 128:(i + 1) * 128, hf0:hf0 + 1024],
                                    x1_f[:, hf0:hf0 + 1024])
        self.evt("a2:done", dX1)
        cP1.close()
        cA.close()

    def _attn_unit(self, tag, s, h, u, qsrc, qtl, kth, vh, praw, pT, bst, dsb,
                   ot_sb, dQT, tM, vV, sS, expand_kv, ksrc, vsrc, dKV,
                   prev_phase_done=None, qs_evts=None, scale_done_evt=None,
                   phase_start_evts=None):
        nc, T, NTS = self.nc, self.T, self.NTS
        PA, PB, PC = self.PA, self.PB, self.PC
        NQR = T // 512
        dB2g = self.sem(f"{tag}_db2g")
        dB2s = self.sem(f"{tag}_db2s")
        with nc.Block(no_gpsimd_drain=True) as b:
            @b.gpsimd
            def _(e):
                if phase_start_evts is not None:
                    for k in phase_start_evts:
                        self.wait(e, k)
                if u >= 2:
                    self.wait(e, f"{tag}:qfree:{u - 2}")
                if qs_evts is not None and u % H == 0:
                    self.wait(e, qs_evts[0])
                    self.wait(e, qs_evts[1])
                if scale_done_evt is not None and u == 0:
                    self.wait(e, scale_done_evt)
                dQT[u % 2].dma(e, qtl[u % 2][:, :], qsrc[s, h])
                if not expand_kv:
                    if u >= 1:
                        self.wait(e, f"{tag}:avdone:{u - 1}")
                        self.wait(e, f"{tag}:stdone:{u - 1}")
                    dKV[0].dma(e, kth[:, :], ksrc[s, h])
                    dKV[1].dma(e, vh[:, :].rearrange("p (kc d) -> p kc d", kc=NTS), vsrc[s, h])

            if expand_kv:
                @b.tensor
                def _(e):
                    if u == 0:
                        self.wait(e, "v:init")
                        self.wait(e, "a0:done_v")
                        self.wait(e, "a0:wcv:49")
                        if qs_evts is not None:
                            for k in qs_evts:
                                self.wait(e, k)
                    if u >= 1:
                        self.wait(e, f"{tag}:exp:{u - 1}:{NTS - 1}")
                    if prev_phase_done is not None and u % H == 0:
                        self.wait(e, prev_phase_done)
                    for off in range(0, T, 512):
                        inst = e.matmul(PA[:, off:off + 512],
                                        self.kexp_b[:, h * 128:(h + 1) * 128],
                                        self.klatT[:, s * T + off:s * T + off + 512],
                                        start=True, stop=True)
                    tM.inc(inst)
                    self.evt(f"{tag}:kexp:{u}", tM)
                @b.vector
                def _(e):
                    self.wait(e, f"{tag}:kexp:{u}")
                    if u >= 1:
                        self.wait(e, f"{tag}:stdone:{u - 1}")
                    vV.inc(e.tensor_copy(kth[:, :], PA[:, 0:T]))
                    self.evt(f"{tag}:kthcp:{u}", vV)
                @b.tensor
                def _(e):
                    self.wait(e, f"{tag}:kthcp:{u}")
                    for kc in range(NTS):
                        inst = e.matmul(PA[:, kc * 128:(kc + 1) * 128],
                                        self.vlatT[:, s * T + kc * 128:s * T + (kc + 1) * 128],
                                        self.vexp_b[:, h * 128:(h + 1) * 128],
                                        start=True, stop=True)
                    tM.inc(inst)
                    self.evt(f"{tag}:vexp:{u}", tM)
                @b.vector
                def _(e):
                    self.wait(e, f"{tag}:vexp:{u}")
                    if u >= 1:
                        self.wait(e, f"{tag}:avdone:{u - 1}")
                    vV.inc(e.tensor_copy(vh[:, :], PA[:, 0:T]))
                    self.evt(f"{tag}:vhdone:{u}", vV)

            for qi in range(NTS):
                @b.tensor
                def _(e, qi=qi):
                    if qi == 0:
                        if expand_kv:
                            self.wait(e, f"{tag}:vhdone:{u}")
                        else:
                            if u == 0:
                                self.wait(e, "v:init")
                            e.wait_ge(dKV[0].h, 16 * (u + 1))
                            e.wait_ge(dKV[1].h, 16 * (u + 1))
                            if u >= 1:
                                self.wait(e, f"{tag}:exp:{u - 1}:{NTS - 1}")
                            if prev_phase_done is not None and u % H == 0:
                                self.wait(e, prev_phase_done)
                        e.wait_ge(dQT[u % 2].h, 16 * (u // 2 + 1))
                    else:
                        self.wait(e, f"{tag}:exp:{u}:{qi - 1}")
                    for kc in range(qi + 1):
                        inst = e.matmul(PA[:, kc * 128:(kc + 1) * 128],
                                        qtl[u % 2][:, qi * 128:(qi + 1) * 128],
                                        kth[:, kc * 128:(kc + 1) * 128], start=True, stop=True)
                    tM.inc(inst)
                    self.evt(f"{tag}:s:{u}:{qi}", tM)
                @b.vector
                def _(e, qi=qi):
                    self.wait(e, f"{tag}:s:{u}:{qi}")
                    if qi >= 1:
                        self.wait(e, f"{tag}:exp:{u}:{qi - 1}")
                    elif u >= 1:
                        self.wait(e, f"{tag}:exp:{u - 1}:{NTS - 1}")
                    vV.inc(e.tensor_tensor(dsb[:, :], PA[:, qi * 128:(qi + 1) * 128],
                                           self.tri_f[:, :], op=ALU.add))
                    self.evt(f"{tag}:dsb:{u}:{qi}", vV)
                    e.drain()
                    e.tensor_reduce(self.stat[:, 8:9], dsb[:, :], AX.X, op=ALU.max)
                    if qi > 0:
                        e.tensor_reduce(self.stat[:, 9:10], PA[:, 0:qi * 128], AX.X, op=ALU.max)
                        e.drain()
                        e.tensor_tensor(self.stat[:, 8:9], self.stat[:, 8:9], self.stat[:, 9:10],
                                        op=ALU.max)
                    e.drain()
                    vV.inc(e.tensor_scalar(self.stat[:, 10:11], self.stat[:, 8:9], -1.0, None,
                                           op0=ALU.mult))
                    self.evt(f"{tag}:max:{u}:{qi}", vV)
                @b.scalar
                def _(e, qi=qi):
                    self.wait(e, f"{tag}:max:{u}:{qi}")
                    if qi > 0:
                        e.activation(praw[:, 0:qi * 128], PA[:, 0:qi * 128], AF.Exp,
                                     bias=self.stat[:, 10:11], accum_out=self.stat[:, 11:12])
                    inst = e.activation(praw[:, qi * 128:(qi + 1) * 128], dsb[:, :], AF.Exp,
                                        bias=self.stat[:, 10:11], accum_out=self.stat[:, 12:13])
                    sS.inc(inst)
                    self.evt(f"{tag}:exp:{u}:{qi}", sS)
                @b.vector
                def _(e, qi=qi):
                    self.wait(e, f"{tag}:exp:{u}:{qi}")
                    if qi > 0:
                        vV.inc(e.tensor_tensor(self.stat[:, 13:14], self.stat[:, 11:12],
                                               self.stat[:, 12:13], op=ALU.add))
                    else:
                        vV.inc(e.tensor_copy(self.stat[:, 13:14], self.stat[:, 12:13]))
                    self.evt(f"{tag}:l:{u}:{qi}", vV)
                @b.scalar
                def _(e, qi=qi):
                    self.wait(e, f"{tag}:l:{u}:{qi}")
                    sS.inc(e.activation(self.stat[:, 19:20], self.stat[:, 13:14], AF.Ln))
                    self.evt(f"{tag}:lnl:{u}:{qi}", sS)
                @b.vector
                def _(e, qi=qi):
                    self.wait(e, f"{tag}:lnl:{u}:{qi}")
                    vV.inc(e.tensor_tensor(bst["bstage"][:, qi:qi + 1], self.stat[:, 10:11],
                                           self.stat[:, 19:20], op=ALU.subtract))
                    self.evt(f"{tag}:bcol:{u}:{qi}", vV)

            @b.tensor
            def _(e):
                self.wait(e, f"{tag}:bcol:{u}:{NTS - 1}")
                if u >= 1:
                    self.wait(e, f"{tag}:pevdone:{u - 1}")
                inst = e.transpose(PB[0:NTS, 0:128], bst["bstage"][:, 0:NTS], self.eye_f[:, :])
                tM.inc(inst)
                self.evt(f"{tag}:btr:{u}", tM)
            @b.vector
            def _(e):
                self.wait(e, f"{tag}:btr:{u}")
                e.tensor_copy(bst["btmp"][0:NTS, :], PB[0:NTS, 0:128])
                e.drain()
                e.tensor_copy(bst["bhi"][0:NTS, :], bst["btmp"][0:NTS, :])
                e.drain()
                vV.inc(e.tensor_tensor(bst["blo"][0:NTS, :], bst["btmp"][0:NTS, :], bst["bhi"][0:NTS, :],
                                       op=ALU.subtract))
                self.evt(f"{tag}:bsplit:{u}", vV)
            @b.gpsimd
            def _(e):
                self.wait(e, f"{tag}:bsplit:{u}")
                if u >= 1:
                    self.wait(e, f"{tag}:b2:{u - 1}")
                dB2g.dma(e, self.b2_scr[0:1, :].rearrange("one (c f) -> (one c) f", c=NTS),
                        bst["bhi"][0:NTS, :])
                dB2g.dma(e, self.b2_scr[1:2, :].rearrange("one (c f) -> (one c) f", c=NTS),
                        bst["blo"][0:NTS, :])
                self.evt(f"{tag}:b2w:{u}", dB2g)
            @b.sync
            def _(e):
                self.wait(e, f"{tag}:b2w:{u}")
                if u >= 1:
                    self.wait(e, f"{tag}:stdone:{u - 1}")
                dB2s.dma(e, bst["b2row"][:, :], self.b2_scr[:, :])
                self.evt(f"{tag}:b2:{u}", dB2s)

            n_idx = 0
            tiles = []
            for qr in range(NQR):
                for kc in range(4 * qr + 4):
                    tiles.append((qr, kc, n_idx))
                    n_idx += 1
            ntile = len(tiles)

            @b.tensor
            def _(e):
                for (qr, kc, n) in tiles:
                    par = n % 2
                    if n >= 1:
                        pqr, pkc, pn_ = tiles[n - 1]
                        self.wait(e, f"{tag}:pexp:{u}:{n - 1}")
                        if pkc == 0:
                            if pqr == 0:
                                if u >= 1:
                                    self.wait(e, f"{tag}:otq:{u - 1}:{NQR - 1}")
                            else:
                                self.wait(e, f"{tag}:otq:{u}:{pqr - 1}")
                        inst = e.matmul(PC[:, 0:512],
                                        vh[:, pkc * 128:(pkc + 1) * 128],
                                        pT[pn_ % 2][:, :],
                                        start=(pkc == 0), stop=(pkc == 4 * pqr + 3))
                        if pkc == 4 * pqr + 3:
                            tM.inc(inst)
                            self.evt(f"{tag}:avq:{u}:{pqr}", tM)
                    if n == 0:
                        self.wait(e, f"{tag}:b2:{u}")
                        self.wait(e, f"{tag}:bsplit:{u}")
                    if n >= 2:
                        self.wait(e, f"{tag}:pev:{u}:{n - 2}")
                    e.matmul(PB[:, par * 512:(par + 1) * 512],
                             kth[:, kc * 128:(kc + 1) * 128],
                             qtl[u % 2][:, qr * 512:(qr + 1) * 512],
                             start=True, stop=False)
                    inst = e.matmul(PB[:, par * 512:(par + 1) * 512],
                                    self.ones2_b[:, :],
                                    bst["b2row"][:, qr * 512:(qr + 1) * 512],
                                    start=False, stop=True)
                    tM.inc(inst)
                    self.evt(f"{tag}:st2:{u}:{n}", tM)
                    if n == ntile - 1:
                        self.evt(f"{tag}:stdone:{u}", tM)
                        self.evt(f"{tag}:qfree:{u}", tM)
                (qr, kc, n) = tiles[-1]
                self.wait(e, f"{tag}:pexp:{u}:{n}")
                inst = e.matmul(PC[:, 0:512], vh[:, kc * 128:(kc + 1) * 128],
                                pT[n % 2][:, :], start=(kc == 0), stop=True)
                tM.inc(inst)
                self.evt(f"{tag}:avq:{u}:{NQR - 1}", tM)
                self.evt(f"{tag}:avdone:{u}", tM)

            @b.vector
            def _(e):
                sc = [bst["sc0"], bst["sc1"]]
                def emit_otq(qr):
                    self.wait(e, f"{tag}:avq:{u}:{qr}")
                    if u == H and qr == 0 and tag == "a1":
                        self.wait(e, "a2:proj:0:1:%d" % (NTS - 1))
                    if u == H and qr == 0 and tag == "a4":
                        self.wait(e, "a5:proj:0:1:%d" % (NTS - 1))
                    vV.inc(e.tensor_copy(ot_sb[:, h * T + qr * 512:h * T + (qr + 1) * 512],
                                         PC[:, 0:512]))
                    self.evt(f"{tag}:otq:{u}:{qr}", vV)
                for (qr, kc, n) in tiles:
                    par = n % 2
                    self.wait(e, f"{tag}:st2:{u}:{n}")
                    if n >= 2:
                        self.wait(e, f"{tag}:pexp:{u}:{n - 2}")
                    if kc >= 4 * qr:
                        j = kc - 4 * qr
                        vV.inc(e.tensor_tensor(sc[par][:, :], PB[:, par * 512:(par + 1) * 512],
                                               self.trit_b[:, j * 512:(j + 1) * 512],
                                               op=ALU.add))
                    else:
                        vV.inc(e.tensor_copy(sc[par][:, :], PB[:, par * 512:(par + 1) * 512]))
                    self.evt(f"{tag}:pev:{u}:{n}", vV)
                    if n == ntile - 1:
                        self.evt(f"{tag}:pevdone:{u}", vV)
                    if qr > 0 and kc == 0:
                        emit_otq(qr - 1)
                emit_otq(NQR - 1)

            @b.scalar
            def _(e):
                sc = [bst["sc0"], bst["sc1"]]
                for (qr, kc, n) in tiles:
                    par = n % 2
                    self.wait(e, f"{tag}:pev:{u}:{n}")
                    inst = e.activation(pT[par][:, :], sc[par][:, :], AF.Exp)
                    sS.inc(inst)
                    self.evt(f"{tag}:pexp:{u}:{n}", sS)

            if h == H - 1:
                self.evt(f"{tag}:otdone:{s}", vV)

    def p_ffn(self, src, dst, tag):
        nc, T, NT = self.nc, self.T, self.NT
        PA, PB, PC = self.PA, self.PB, self.PC
        if NT >= 20:
            sizes = [7, 7, 7, 7, NT - 28]
        else:
            tpg = max(1, NT // 4)
            sizes = []
            r = NT
            while r > 0:
                sizes.append(min(tpg, r))
                r -= min(tpg, r)
        TPGm = max(sizes) * 128
        G = len(sizes)

        c = ExitStack()
        sb = lambda n, sh, d: c.enter_context(nc.sbuf_tensor(n, sh, d))
        xnt_g = sb(f"{tag}_xnt", [128, 16 * TPGm], BF16)
        y_g = sb(f"{tag}_y", [128, max(sizes) * D], F32)
        h_b = sb(f"{tag}_h", [128, NI * TPGm], BF16)
        w1b = [sb(f"{tag}_w1{i}", [128, D], BF16) for i in range(2)]
        w3b = [sb(f"{tag}_w3{i}", [128, D], BF16) for i in range(2)]
        w2b = [sb(f"{tag}_w2{i}", [128, NI * 1024], BF16) for i in range(2)]
        xs_f = [sb(f"{tag}_xs{i}", [128, D], F32) for i in range(2)]
        xn_b = [sb(f"{tag}_xn{i}", [128, D], BF16) for i in range(2)]
        sg_f = sb(f"{tag}_sg", [128, TPGm], F32)
        tf_f = sb(f"{tag}_tf", [128, TPGm], F32)
        rt = sb(f"{tag}_rt", [128, 112], F32)
        comb = sb(f"{tag}_cb", [128, max(sizes) * E], F32)
        wgt_f = sb(f"{tag}_wgf", [128, 16 * E], F32)
        wgt_b = sb(f"{tag}_wgb", [128, 16 * E], BF16)

        dXF = [self.sem(f"{tag}_dx{i}") for i in range(2)]
        dW1 = [self.sem(f"{tag}_dw1{i}") for i in range(2)]
        dW3 = [self.sem(f"{tag}_dw3{i}") for i in range(2)]
        dW2 = [self.sem(f"{tag}_dw2{i}") for i in range(2)]
        dIY = self.sem(f"{tag}_diy")
        dYO = self.sem(f"{tag}_dyo")
        dG = self.sem(f"{tag}_dg")
        tF = self.sem(f"{tag}_t")
        vF = self.sem(f"{tag}_v")
        sF = self.sem(f"{tag}_s")

        if tag == "f1":
            entry = ["a2:x1ev:1:1:%d" % (self.NTS - 1), "a2:proj:1:1:%d" % (self.NTS - 1), "a2:done"]
        else:
            entry = ["a5:x1ev:1:1:%d" % (self.NTS - 1), "a5:proj:1:1:%d" % (self.NTS - 1), "a5:done"]

        with nc.Block(no_gpsimd_drain=True) as b:
            @b.sync
            def _(e):
                for k in entry:
                    self.wait(e, k)
                dG.dma(e, wgt_f[:, :].rearrange("p (c g) -> p c g", c=16),
                       self.wgt.rearrange("(c p) g -> p c g", p=128))
            @b.vector
            def _(e):
                e.wait_ge(dG.h, 16)
                vF.inc(e.tensor_copy(wgt_b[:, :], wgt_f[:, :]))
                self.evt(f"{tag}:wg", vF)

        kk = 0
        gt0 = 0
        hist_w13 = {0: None, 1: None}
        hist_w2 = {0: None, 1: None}
        prev_g_events = None
        for g, ng in enumerate(sizes):
            TOKg = ng * 128
            for tt in range(ng):
                gi = gt0 + tt
                i_k = kk + tt
                with nc.Block(no_gpsimd_drain=True) as b:
                    @b.sync
                    def _(e):
                        if i_k >= 2:
                            self.wait(e, f"{tag}:ln:{i_k - 2}")
                        dXF[i_k % 2].dma(e, xs_f[i_k % 2][:, :], src[gi * 128:(gi + 1) * 128, :])
                    @b.gpsimd
                    def _(e):
                        if g == 0 and tt == 0:
                            for k in entry:
                                self.wait(e, k)
                        if g > 0 and tt == 0:
                            self.wait(e, f"{tag}:ydone:{g - 1}")
                        dIY.dma(e, y_g[:, tt * D:(tt + 1) * D], src[gi * 128:(gi + 1) * 128, :])
                        self.evt(f"{tag}:yinit:{g}:{tt}", dIY)
                    @b.vector
                    def _(e):
                        e.wait_ge(dXF[i_k % 2].h, 16 * (i_k // 2 + 1))
                        vF.inc(e.tensor_reduce(self.stat[:, 0:1], xs_f[i_k % 2][:, :], AX.X, op=ALU.add))
                        self.evt(f"{tag}:mean0:{i_k}", vF)
                    @b.scalar
                    def _(e):
                        e.wait_ge(dXF[i_k % 2].h, 16 * (i_k // 2 + 1))
                        if i_k >= 1:
                            self.wait(e, f"{tag}:var:{i_k - 1}")
                        e.activation(self.sqt[:, :], xs_f[i_k % 2][:, 0:D // 2], AF.Square,
                                     accum_out=self.stat[:, 1:2])
                        e.drain()
                        sF.inc(e.activation(self.sqt[:, :], xs_f[i_k % 2][:, D // 2:D], AF.Square,
                                            accum_out=self.stat[:, 18:19]))
                        self.evt(f"{tag}:sumsq:{i_k}", sF)
                    @b.vector
                    def _(e):
                        self.wait(e, f"{tag}:sumsq:{i_k}")
                        e.drain()
                        e.tensor_tensor(self.stat[:, 1:2], self.stat[:, 1:2], self.stat[:, 18:19], op=ALU.add)
                        e.tensor_scalar(self.stat[:, 2:3], self.stat[:, 0:1], 1.0 / D, None, op0=ALU.mult)
                        e.drain()
                        e.tensor_tensor(self.stat[:, 3:4], self.stat[:, 2:3], self.stat[:, 2:3], op=ALU.mult)
                        e.drain()
                        vF.inc(e.scalar_tensor_tensor(self.stat[:, 4:5], self.stat[:, 1:2], 1.0 / D,
                                                      self.stat[:, 3:4], op0=ALU.mult, op1=ALU.subtract))
                        self.evt(f"{tag}:var:{i_k}", vF)
                    @b.scalar
                    def _(e):
                        self.wait(e, f"{tag}:var:{i_k}")
                        sF.inc(e.activation(self.stat[:, 5:6], self.stat[:, 4:5], AF.Sqrt,
                                            bias=self.stat[:, 16:17]))
                        self.evt(f"{tag}:std:{i_k}", sF)
                    @b.vector
                    def _(e):
                        self.wait(e, f"{tag}:std:{i_k}")
                        e.drain()
                        e.reciprocal(self.stat[:, 6:7], self.stat[:, 5:6])
                        e.drain()
                        if i_k >= 2:
                            self.wait(e, f"{tag}:xtr:{i_k - 2}")
                        vF.inc(e.tensor_scalar(xn_b[i_k % 2][:, :], xs_f[i_k % 2][:, :], self.stat[:, 2:3],
                                               self.stat[:, 6:7], op0=ALU.subtract, op1=ALU.mult))
                        self.evt(f"{tag}:ln:{i_k}", vF)
                    @b.tensor
                    def _(e):
                        self.wait(e, f"{tag}:ln:{i_k}")
                        if g == 0 and tt == 0:
                            pass
                        if tt == 0 and g > 0:
                            self.wait(e, f"{tag}:yev:{g - 1}:17:1:{sizes[g - 1] - 1}")
                        if tt >= 1:
                            self.wait(e, f"{tag}:xcp:{i_k - 1}")
                        elif g == 0 and kk == 0:
                            pass
                        pbb = PB[:, :].bitcast(BF16)
                        for cch in range(16):
                            inst = e.transpose(pbb[:, cch * 128:(cch + 1) * 128],
                                               xn_b[i_k % 2][:, cch * 128:(cch + 1) * 128], self.eye_b[:, :])
                        tM = tF
                        tM.inc(inst)
                        self.evt(f"{tag}:xtr:{i_k}", tF)
                    @b.vector
                    def _(e):
                        self.wait(e, f"{tag}:xtr:{i_k}")
                        if g > 0 and tt == 0:
                            self.wait(e, f"{tag}:hmmlast:{g - 1}")
                        dstv = xnt_g[:, :].rearrange("p (c t) -> p c t", c=16)[:, :, tt * 128:(tt + 1) * 128]
                        vF.inc(e.tensor_copy(dstv, PB[:, :].bitcast(BF16)))
                        self.evt(f"{tag}:xcp:{i_k}", vF)
                    @b.tensor
                    def _(e):
                        self.wait(e, f"{tag}:xcp:{i_k}")
                        if g == 0 and tt == 0:
                            self.wait(e, f"{tag}:wg")
                        if tt >= 1:
                            self.wait(e, f"{tag}:rexp:{g}:{tt - 1}")
                        elif g > 0:
                            self.wait(e, f"{tag}:rexp:{g - 1}:{sizes[g - 1] - 1}")
                        for cch in range(16):
                            inst = e.matmul(PC[:, 0:E], xnt_g[:, :].rearrange("p (c t) -> p c t", c=16)
                                            [:, cch, tt * 128:(tt + 1) * 128],
                                            wgt_b[:, cch * E:(cch + 1) * E],
                                            start=(cch == 0), stop=(cch == 15))
                        tF.inc(inst)
                        self.evt(f"{tag}:lg:{g}:{tt}", tF)
                    @b.vector
                    def _(e):
                        self.wait(e, f"{tag}:lg:{g}:{tt}")
                        e.tensor_copy(rt[:, 84:84 + E], PC[:, 0:E])
                        e.drain()
                        e.tensor_reduce(self.stat[:, 8:9], rt[:, 84:84 + E], AX.X, op=ALU.max)
                        e.drain()
                        vF.inc(e.tensor_scalar(self.stat[:, 9:10], self.stat[:, 8:9], -1.0, None, op0=ALU.mult))
                        self.evt(f"{tag}:rmax:{g}:{tt}", vF)
                    @b.scalar
                    def _(e):
                        self.wait(e, f"{tag}:rmax:{g}:{tt}")
                        sF.inc(e.activation(rt[:, 0:E], rt[:, 84:84 + E], AF.Exp, bias=self.stat[:, 9:10],
                                            accum_out=self.stat[:, 10:11]))
                        self.evt(f"{tag}:rexp:{g}:{tt}", sF)
                    @b.vector
                    def _(e):
                        self.wait(e, f"{tag}:rexp:{g}:{tt}")
                        e.reciprocal(self.stat[:, 11:12], self.stat[:, 10:11])
                        e.drain()
                        e.tensor_scalar(rt[:, 16:32], rt[:, 0:E], self.stat[:, 11:12], None, op0=ALU.mult)
                        e.drain()
                        for gg in range(4):
                            e.tensor_reduce(rt[:, 32 + gg:33 + gg], rt[:, 16 + 4 * gg:20 + 4 * gg],
                                            AX.X, op=ALU.max)
                        e.drain()
                        e.tensor_reduce(self.stat[:, 12:13], rt[:, 32:36], AX.X, op=ALU.max)
                        e.drain()
                        e.tensor_scalar(rt[:, 36:40], rt[:, 32:36], self.stat[:, 12:13], None, op0=ALU.is_ge)
                        e.drain()
                        e.scalar_tensor_tensor(rt[:, 40:44], rt[:, 36:40], BIG_NEG, rt[:, 32:36],
                                               op0=ALU.mult, op1=ALU.add)
                        e.drain()
                        e.tensor_reduce(self.stat[:, 13:14], rt[:, 40:44], AX.X, op=ALU.max)
                        e.drain()
                        e.tensor_scalar(rt[:, 44:48], rt[:, 32:36], self.stat[:, 13:14], None, op0=ALU.is_ge)
                        e.drain()
                        e.tensor_scalar(rt[:, 48:52], rt[:, 44:48], 1.0, -BIG_NEG, op0=ALU.subtract, op1=ALU.mult)
                        e.drain()
                        for gg in range(4):
                            e.tensor_scalar(rt[:, 52 + 4 * gg:56 + 4 * gg], rt[:, 16 + 4 * gg:20 + 4 * gg],
                                            rt[:, 48 + gg:49 + gg], None, op0=ALU.add)
                        e.drain()
                        e.tensor_copy(rt[:, 68:84], rt[:, 52:68])
                        e.drain()
                        for rr in range(1, 5):
                            e.tensor_reduce(self.stat[:, 14:15], rt[:, 68:84], AX.X, op=ALU.max)
                            e.drain()
                            if rr < 4:
                                e.tensor_scalar(rt[:, 84:100][:, 0:16] if False else rt[:, 0:16],
                                                rt[:, 68:84], self.stat[:, 14:15], None, op0=ALU.is_ge)
                                e.drain()
                                e.scalar_tensor_tensor(rt[:, 68:84], rt[:, 0:16], BIG_NEG, rt[:, 68:84],
                                                       op0=ALU.mult, op1=ALU.add)
                                e.drain()
                        e.tensor_scalar(rt[:, 0:16], rt[:, 52:68], self.stat[:, 14:15], None, op0=ALU.is_ge)
                        e.drain()
                        vF.inc(e.tensor_tensor(comb[:, tt * E:(tt + 1) * E], rt[:, 0:16], rt[:, 16:32],
                                               op=ALU.mult))
                        self.evt(f"{tag}:comb:{g}:{tt}", vF)

            for ex in range(NEX):
                with nc.Block(no_gpsimd_drain=True) as b:
                    @b.sync
                    def _(e):
                        for i in range(NI):
                            sl = i % 2
                            prev = hist_w13[sl]
                            if prev is not None:
                                self.wait(e, prev)
                            dW1[sl].dma(e, w1b[sl][:, :].rearrange("p (c j) -> p c j", c=16),
                                        self.wc1[ex, i])
                            self.evt(f"{tag}:w1ld:{g}:{ex}:{i}", dW1[sl])
                            dW3[sl].dma(e, w3b[sl][:, :].rearrange("p (c j) -> p c j", c=16),
                                        self.wc3[ex, i])
                            self.evt(f"{tag}:w3ld:{g}:{ex}:{i}", dW3[sl])
                            hist_w13[sl] = f"{tag}:hmm:{g}:{ex}:{i}"
                        for hf in range(2):
                            sl = (2 * ex + hf) % 2
                            prev = hist_w2[sl]
                            if prev is not None:
                                self.wait(e, prev)
                            dW2[sl].dma(e, w2b[sl][:, :].rearrange("p (i j) -> p i j", i=NI),
                                        self.wc2[ex, hf])
                            self.evt(f"{tag}:w2ld:{g}:{ex}:{hf}", dW2[sl])
                            hist_w2[sl] = f"{tag}:w2mm:{g}:{ex}:{hf}:{ng - 1}"

                    @b.tensor
                    def _(e):
                        for i in range(NI):
                            sl = i % 2
                            self.wait(e, f"{tag}:w1ld:{g}:{ex}:{i}")
                            self.wait(e, f"{tag}:w3ld:{g}:{ex}:{i}")
                            if ex == 0 and i == 0:
                                self.wait(e, f"{tag}:comb:{g}:{ng - 1}")
                                if g > 0:
                                    self.wait(e, f"{tag}:yev:{g - 1}:17:1:{sizes[g - 1] - 1}")
                            if ex == 0 and i == 1:
                                self.wait(e, f"{tag}:comb:{g}:{ng - 1}")
                                self.wait(e, f"{tag}:xcp:{kk + ng - 1}")
                            if i >= 2:
                                self.wait(e, f"{tag}:hb:{g}:{ex}:{i - 2}")
                            elif ex > 0:
                                self.wait(e, f"{tag}:yev:{g}:{ex - 1}:1:{ng - 1}")
                            elif g > 0 and ex == 0:
                                self.wait(e, f"{tag}:yev:{g - 1}:17:1:{sizes[g - 1] - 1}")
                            h1p = PA[:, 0:TOKg] if i % 2 == 0 else PB[:, 0:TOKg]
                            h3p = PA[:, 1024:1024 + TOKg] if i % 2 == 0 else PC[:, 0:TOKg]
                            xv = xnt_g[:, :].rearrange("p (c t) -> p c t", c=16)
                            for off in range(0, TOKg, 512):
                                w = min(512, TOKg - off)
                                for cch in range(16):
                                    e.matmul(h1p[:, off:off + w], w1b[sl][:, cch * 128:(cch + 1) * 128],
                                             xv[:, cch, off:off + w],
                                             start=(cch == 0), stop=(cch == 15))
                            for off in range(0, TOKg, 512):
                                w = min(512, TOKg - off)
                                for cch in range(16):
                                    inst = e.matmul(h3p[:, off:off + w], w3b[sl][:, cch * 128:(cch + 1) * 128],
                                                    xv[:, cch, off:off + w],
                                                    start=(cch == 0), stop=(cch == 15))
                            tF.inc(inst)
                            self.evt(f"{tag}:hmm:{g}:{ex}:{i}", tF)
                    @b.vector
                    def _(e):
                        for i in range(NI):
                            self.wait(e, f"{tag}:hmm:{g}:{ex}:{i}")
                            e.drain()
                            h1p = PA[:, 0:TOKg] if i % 2 == 0 else PB[:, 0:TOKg]
                            vF.inc(e.tensor_copy(tf_f[:, 0:TOKg], h1p))
                            self.evt(f"{tag}:hev:{g}:{ex}:{i}", vF)
                            self.wait(e, f"{tag}:sg:{g}:{ex}:{i}")
                            h3p = PA[:, 1024:1024 + TOKg] if i % 2 == 0 else PC[:, 0:TOKg]
                            e.tensor_tensor(tf_f[:, 0:TOKg], tf_f[:, 0:TOKg], sg_f[:, 0:TOKg], op=ALU.mult)
                            e.drain()
                            if ex == 0 and i == 0 and g > 0:
                                self.wait(e, f"{tag}:w2mm:{g - 1}:17:1:{sizes[g - 1] - 1}")
                            if ex > 0 and i == 0:
                                self.wait(e, f"{tag}:w2mm:{g}:{ex - 1}:1:{ng - 1}")
                            vF.inc(e.tensor_tensor(h_b[:, i * TPGm:i * TPGm + TOKg], tf_f[:, 0:TOKg],
                                                   h3p, op=ALU.mult))
                            self.evt(f"{tag}:hb:{g}:{ex}:{i}", vF)
                    @b.scalar
                    def _(e):
                        for i in range(NI):
                            self.wait(e, f"{tag}:hev:{g}:{ex}:{i}")
                            if i >= 1:
                                self.wait(e, f"{tag}:hb:{g}:{ex}:{i - 1}")
                            elif ex > 0:
                                self.wait(e, f"{tag}:hb:{g}:{ex - 1}:{NI - 1}")
                            elif g > 0:
                                self.wait(e, f"{tag}:hb:{g - 1}:{NEX - 1}:{NI - 1}")
                            sF.inc(e.activation(sg_f[:, 0:TOKg], tf_f[:, 0:TOKg], AF.Sigmoid))
                            self.evt(f"{tag}:sg:{g}:{ex}:{i}", sF)
                    @b.tensor
                    def _(e):
                        for hf in range(2):
                            sl = (2 * ex + hf) % 2
                            for tt in range(ng):
                                m = hf * ng + tt
                                if hf == 0 and tt == 0:
                                    self.wait(e, f"{tag}:hb:{g}:{ex}:{NI - 1}")
                                    self.wait(e, f"{tag}:w2ld:{g}:{ex}:0")
                                if hf == 1 and tt == 0:
                                    self.wait(e, f"{tag}:w2ld:{g}:{ex}:1")
                                if m >= 2:
                                    hfp, ttp = divmod(m - 2, ng)
                                    self.wait(e, f"{tag}:yev:{g}:{ex}:{hfp}:{ttp}")
                                wp = PA[:, 0:1024] if m % 2 == 0 else PB[:, 0:1024]
                                for i in range(NI):
                                    for nn2 in range(2):
                                        inst = e.matmul(wp[:, nn2 * 512:(nn2 + 1) * 512],
                                                        h_b[:, i * TPGm + tt * 128:i * TPGm + (tt + 1) * 128],
                                                        w2b[sl][:, i * 1024 + nn2 * 512:i * 1024 + (nn2 + 1) * 512],
                                                        start=(i == 0), stop=(i == NI - 1))
                                tF.inc(inst)
                                self.evt(f"{tag}:w2mm:{g}:{ex}:{hf}:{tt}", tF)
                    @b.vector
                    def _(e):
                        for hf in range(2):
                            for tt in range(ng):
                                m = hf * ng + tt
                                self.wait(e, f"{tag}:w2mm:{g}:{ex}:{hf}:{tt}")
                                if ex == 0:
                                    self.wait(e, f"{tag}:yinit:{g}:{ng - 1}")
                                wp = PA[:, 0:1024] if m % 2 == 0 else PB[:, 0:1024]
                                ysl = y_g[:, tt * D + hf * 1024:tt * D + hf * 1024 + 1024]
                                if ex < E:
                                    inst = e.scalar_tensor_tensor(ysl, wp, comb[:, tt * E + ex:tt * E + ex + 1],
                                                                  ysl, op0=ALU.mult, op1=ALU.add)
                                else:
                                    inst = e.tensor_tensor(ysl, wp, ysl, op=ALU.add)
                                vF.inc(inst)
                                self.evt(f"{tag}:yev:{g}:{ex}:{hf}:{tt}", vF)
                if ex == NEX - 1:
                    self.evt(f"{tag}:hmmlast:{g}", tF)

            with nc.Block(no_gpsimd_drain=True) as b:
                @b.gpsimd
                def _(e):
                    for tt in range(ng):
                        gi = gt0 + tt
                        self.wait(e, f"{tag}:yev:{g}:17:1:{tt}")
                        self.wait(e, f"{tag}:yev:{g}:17:0:{tt}")
                        dYO.dma(e, dst[gi * 128:(gi + 1) * 128, :], y_g[:, tt * D:(tt + 1) * D])
                    self.evt(f"{tag}:ydone:{g}", dYO)
            kk += ng
            gt0 += ng

        self.evt(f"{tag}:done", dYO)
        self._ffn_last = {"v": f"{tag}:yev:{G - 1}:17:1:{sizes[-1] - 1}",
                          "t": f"{tag}:w2mm:{G - 1}:17:1:{sizes[-1] - 1}"}
        self.out_sems.append(dYO)
        c.close()

    def p_attn_b(self):
        nc, T, NTS, NT = self.nc, self.T, self.NTS, self.NT
        PA, PB, PC = self.PA, self.PB, self.PC
        SCL = ATTN_B_SCALE
        cB = ExitStack()
        sbB = lambda n, sh, d: cB.enter_context(nc.sbuf_tensor(n, sh, d))
        bias_b = sbB("b0_bias", [1, 4 * D], BF16)
        wstg = [sbB(f"b0_ws{i}", [128, D], F32) for i in range(2)]

        c0 = ExitStack()
        sb = lambda n, sh, d: c0.enter_context(nc.sbuf_tensor(n, sh, d))
        x_f = [sb(f"b0_x{i}", [128, D], F32) for i in range(2)]
        xn_b = [sb(f"b0_xn{i}", [128, D], BF16) for i in range(2)]
        xnt_b = [sb(f"b0_xnt{i}", [128, D], BF16) for i in range(2)]
        qv_b = sb("b0_qv", [128, D], BF16)
        qst_b = [sb(f"b0_qs{i}", [128, D], BF16) for i in range(2)]
        w_b = sb("b0_wb", [128, 16 * D], BF16)
        bias_f = sb("b0_biasf", [1, 4 * D], F32)

        dB = self.sem("b0_db")
        dW = [self.sem(f"b0_dw{i}") for i in range(2)]
        dX = [self.sem(f"b0_dx{i}") for i in range(2)]
        dXNT = self.sem("b0_dxnt")
        dXL = [self.sem(f"b0_dxl{i}") for i in range(2)]
        dSC = [self.sem(f"b0_dsc{i}") for i in range(2)]
        tM = self.sem("b0_t")
        vV = self.sem("b0_v")
        sS = self.sem("b0_s")

        fl = self._ffn_last
        with nc.Block(no_gpsimd_drain=True) as b:
            @b.sync
            def _(e):
                self.wait(e, "f1:done")
                self.wait(e, fl["v"])
                self.wait(e, fl["t"])
                for i, src in enumerate((self.bq, self.bk, self.bv, self.bo)):
                    dB.dma(e, bias_f[0:1, i * D:(i + 1) * D], src.unsqueeze(0))
            @b.vector
            def _(e):
                e.wait_ge(dB.h, 16 * 4)
                vV.inc(e.tensor_copy(bias_b[0:1, :], bias_f[0:1, :]))
                self.evt("b0:bias", vV)

        wsrcs = [self.wqt, self.wkt, self.wvt]
        for ps in range(3):
            with nc.Block(no_gpsimd_drain=True) as b:
                @b.sync
                def _(e):
                    for j in range(16):
                        jj = ps * 16 + j
                        if jj >= 2:
                            self.wait(e, f"b0:wcv:{jj - 2}")
                        dW[j % 2].dma(e, wstg[j % 2][:, :], wsrcs[ps][j * 128:(j + 1) * 128, :])
                @b.vector
                def _(e):
                    for j in range(16):
                        jj = ps * 16 + j
                        e.wait_ge(dW[j % 2].h, 16 * (jj // 2 + 1))
                        if ps > 0 and j == 0:
                            self.wait(e, f"b0:mm:{ps - 1}:{NT - 1}")
                        vV.inc(e.tensor_copy(w_b[:, j * D:(j + 1) * D], wstg[j % 2][:, :]))
                        self.evt(f"b0:wcv:{ps * 16 + j}", vV)

            for i in range(NT):
                s, tr = divmod(i, NTS)
                t0 = tr * 128
                with nc.Block(no_gpsimd_drain=True) as b:
                    if ps == 0:
                        @b.sync
                        def _(e):
                            if i >= 2:
                                self.wait(e, f"b0:ln:{i - 2}")
                            dX[i % 2].dma(e, x_f[i % 2][:, :], self.x2_scr[i * 128:(i + 1) * 128, :])
                        @b.vector
                        def _(e):
                            e.wait_ge(dX[i % 2].h, 16 * (i // 2 + 1))
                            vV.inc(e.tensor_reduce(self.stat[:, 0:1], x_f[i % 2][:, :], AX.X, op=ALU.add))
                            self.evt(f"b0:mean0:{i}", vV)
                        @b.scalar
                        def _(e):
                            e.wait_ge(dX[i % 2].h, 16 * (i // 2 + 1))
                            if i >= 1:
                                self.wait(e, f"b0:var:{i - 1}")
                            e.activation(self.sqt[:, :], x_f[i % 2][:, 0:D // 2], AF.Square,
                                         accum_out=self.stat[:, 1:2])
                            e.drain()
                            sS.inc(e.activation(self.sqt[:, :], x_f[i % 2][:, D // 2:D], AF.Square,
                                                accum_out=self.stat[:, 18:19]))
                            self.evt(f"b0:sumsq:{i}", sS)
                        @b.vector
                        def _(e):
                            self.wait(e, f"b0:sumsq:{i}")
                            e.drain()
                            e.tensor_tensor(self.stat[:, 1:2], self.stat[:, 1:2], self.stat[:, 18:19], op=ALU.add)
                            e.tensor_scalar(self.stat[:, 2:3], self.stat[:, 0:1], 1.0 / D, None, op0=ALU.mult)
                            e.drain()
                            e.tensor_tensor(self.stat[:, 3:4], self.stat[:, 2:3], self.stat[:, 2:3], op=ALU.mult)
                            e.drain()
                            vF = vV
                            vF.inc(e.scalar_tensor_tensor(self.stat[:, 4:5], self.stat[:, 1:2], 1.0 / D,
                                                          self.stat[:, 3:4], op0=ALU.mult, op1=ALU.subtract))
                            self.evt(f"b0:var:{i}", vV)
                        @b.scalar
                        def _(e):
                            self.wait(e, f"b0:var:{i}")
                            sS.inc(e.activation(self.stat[:, 5:6], self.stat[:, 4:5], AF.Sqrt,
                                                bias=self.stat[:, 16:17]))
                            self.evt(f"b0:std:{i}", sS)
                        @b.vector
                        def _(e):
                            self.wait(e, f"b0:std:{i}")
                            e.drain()
                            e.reciprocal(self.stat[:, 6:7], self.stat[:, 5:6])
                            e.drain()
                            if i >= 2:
                                self.wait(e, f"b0:xtr:{i - 2}")
                            vV.inc(e.tensor_scalar(xn_b[i % 2][:, :], x_f[i % 2][:, :], self.stat[:, 2:3],
                                                   self.stat[:, 6:7], op0=ALU.subtract, op1=ALU.mult))
                            self.evt(f"b0:ln:{i}", vV)
                        @b.tensor
                        def _(e):
                            self.wait(e, f"b0:ln:{i}")
                            if i >= 1:
                                self.wait(e, f"b0:qst:0:{i - 1}")
                            pbb = PB[:, :].bitcast(BF16)
                            for cch in range(16):
                                inst = e.transpose(pbb[:, cch * 128:(cch + 1) * 128],
                                                   xn_b[i % 2][:, cch * 128:(cch + 1) * 128],
                                                   self.eye_b[:, :])
                            tM.inc(inst)
                            self.evt(f"b0:xtr:{i}", tM)
                        @b.vector
                        def _(e):
                            self.wait(e, f"b0:xtr:{i}")
                            if i >= 2:
                                self.wait(e, f"b0:mm:0:{i - 2}")
                            vV.inc(e.tensor_copy(xnt_b[i % 2][:, :], PB[:, :].bitcast(BF16)))
                            self.evt(f"b0:xcp:{i}", vV)
                        @b.gpsimd
                        def _(e):
                            self.wait(e, f"b0:xcp:{i}")
                            e.wait_ge(dXNT.h, 16 * i)
                            dXNT.dma(e, self.xnt_scr[i], xnt_b[i % 2][:, :])
                            self.evt(f"b0:xnts:{i}", dXNT)
                    else:
                        @b.sync
                        def _(e):
                            if i >= 2:
                                self.wait(e, f"b0:mm:{ps}:{i - 2}")
                            else:
                                self.wait(e, f"b0:mm:{ps - 1}:{NT - 2 + i}")
                                if ps == 1:
                                    self.wait(e, f"b0:xnts:{NT - 2 + i}")
                            self.wait(e, f"b0:xnts:{i}")
                            dXL[i % 2].dma(e, xnt_b[i % 2][:, :], self.xnt_scr[i])

                    @b.tensor
                    def _(e):
                        if ps == 0:
                            self.wait(e, f"b0:xcp:{i}")
                        else:
                            e.wait_ge(dXL[i % 2].h, 16 * ((ps - 1) * NT + i + 2) // 2 * 1) if False else None
                            e.wait_ge(dXL[i % 2].h, 16 * (((ps - 1) * NT + i) // 2 + 1))
                        if i == 0:
                            self.wait(e, f"b0:wcv:{ps * 16 + 15}")
                            self.wait(e, "b0:bias")
                        if i >= 1:
                            self.wait(e, f"b0:qev:{ps}:{i - 1}")
                        for nq in range(4):
                            for cch in range(16):
                                e.matmul(PA[:, nq * 512:(nq + 1) * 512],
                                         xnt_b[i % 2][:, cch * 128:(cch + 1) * 128],
                                         w_b[:, cch * D + nq * 512:cch * D + (nq + 1) * 512],
                                         start=(cch == 0), stop=False)
                            inst = e.matmul(PA[:, nq * 512:(nq + 1) * 512], self.ones_b[0:1, :],
                                            bias_b[0:1, ps * D + nq * 512:ps * D + (nq + 1) * 512],
                                            start=False, stop=True)
                        tM.inc(inst)
                        self.evt(f"b0:mm:{ps}:{i}", tM)
                    @b.vector
                    def _(e):
                        self.wait(e, f"b0:mm:{ps}:{i}")
                        if i >= 1 and ps < 2:
                            self.wait(e, f"b0:tr2:{ps}:{i - 1}")
                        elif i >= 1:
                            self.wait(e, f"b0:qst:{ps}:{i - 1}")
                        if ps < 2:
                            vV.inc(e.tensor_scalar(qv_b[:, :], PA[:, :], SCL, None, op0=ALU.mult))
                        else:
                            vV.inc(e.tensor_copy(qv_b[:, :], PA[:, :]))
                        self.evt(f"b0:qev:{ps}:{i}", vV)
                    if ps < 2:
                        @b.tensor
                        def _(e):
                            self.wait(e, f"b0:qev:{ps}:{i}")
                            if ps > 0 or True:
                                self.wait(e, f"b0:xcp:{i}") if ps == 0 else None
                            if i >= 1:
                                self.wait(e, f"b0:qst:{ps}:{i - 1}")
                            elif ps > 0:
                                self.wait(e, f"b0:qst:{ps - 1}:{NT - 1}")
                            pbb = PB[:, :].bitcast(BF16)
                            for h in range(H):
                                inst = e.transpose(pbb[:, h * 128:(h + 1) * 128],
                                                   qv_b[:, h * 128:(h + 1) * 128], self.eye_b[:, :])
                            tM.inc(inst)
                            self.evt(f"b0:tr2:{ps}:{i}", tM)
                        @b.vector
                        def _(e):
                            self.wait(e, f"b0:tr2:{ps}:{i}")
                            kki = ps * NT + i
                            if kki >= 2:
                                e.wait_ge(dSC[kki % 2].h, 16 * 16 * (kki // 2))
                            vV.inc(e.tensor_copy(qst_b[kki % 2][:, :], PB[:, :].bitcast(BF16)))
                            self.evt(f"b0:qst:{ps}:{i}", vV)
                        @b.gpsimd
                        def _(e):
                            self.wait(e, f"b0:qst:{ps}:{i}")
                            dscr = self.qt_scr if ps == 0 else self.kt_scr
                            kki = ps * NT + i
                            for h in range(H):
                                dSC[kki % 2].dma(e, dscr[s, h, :, t0:t0 + 128],
                                                 qst_b[kki % 2][:, h * 128:(h + 1) * 128])
                    else:
                        @b.tensor
                        def _(e):
                            pass
                        @b.vector
                        def _(e):
                            self.wait(e, f"b0:qev:{ps}:{i}")
                            kki = ps * NT + i
                            if kki >= 2:
                                e.wait_ge(dSC[kki % 2].h, 16 * 16 * (kki // 2))
                            vV.inc(e.tensor_copy(qst_b[kki % 2][:, :], qv_b[:, :]))
                            self.evt(f"b0:qst:{ps}:{i}", vV)
                        @b.gpsimd
                        def _(e):
                            self.wait(e, f"b0:qst:{ps}:{i}")
                            kki = ps * NT + i
                            for h in range(H):
                                dSC[kki % 2].dma(e, self.v_scr[s, h, :, tr, :],
                                                 qst_b[kki % 2][:, h * 128:(h + 1) * 128])
                if i == NT - 1:
                    self.evt(f"b0:scA:{ps}:{i // NTS}", dSC[0])
                    self.evt(f"b0:scB:{ps}:{i // NTS}", dSC[1])
                if tr == NTS - 1:
                    self.evt(f"b0:scA:{ps}:{s}", dSC[0])
                    self.evt(f"b0:scB:{ps}:{s}", dSC[1])

        self.evt("b0:done_t", tM)
        self.evt("b0:done_v", vV)
        c0.close()

        cP1 = ExitStack()
        sb = lambda n, sh, d: cP1.enter_context(nc.sbuf_tensor(n, sh, d))
        qtl = [sb(f"a4_q{i}", [128, T], BF16) for i in range(2)]
        kth = sb("a4_kth", [128, T], BF16)
        vh = sb("a4_vh", [128, T], BF16)
        praw = sb("a4_praw", [128, T], BF16)
        pT = [sb(f"a4_pt{i}", [128, 512], BF16) for i in range(2)]
        bst = {"bstage": sb("a4_bstage", [128, 16], F32),
               "btmp": sb("a4_btmp", [16, 128], F32),
               "bhi": sb("a4_bhi", [16, 128], BF16),
               "blo": sb("a4_blo", [16, 128], BF16),
               "b2row": sb("a4_b2row", [2, T], BF16),
               "sc0": sb("a4_sc0", [128, 512], F32),
               "sc1": sb("a4_sc1", [128, 512], F32)}
        dsb = sb("a4_dsb", [128, 128], F32)
        ot_sb = sb("a4_ot", [128, H * T], BF16)
        pw_b = sb("a4_pw", [128, 16 * 1024], BF16)
        xr_f = sb("a4_xr", [128, D], F32)
        x1_f = sb("a4_x1", [128, D], F32)

        dQT = [self.sem(f"a4_dqt{i}") for i in range(2)]
        dKV = [self.sem(f"a4_dkv{i}") for i in range(2)]
        dPW = [self.sem(f"a4_dpw{i}") for i in range(2)]
        dXR = self.sem("a4_dxr")
        dX1 = self.sem("a4_dx1")

        for s in range(2):
            for h in range(H):
                u = s * H + h
                pstart = None
                if u == 0:
                    pstart = ("b0:done_t", "b0:done_v", "b0:xnts:%d" % (NT - 1),
                              "b0:scA:0:1", "b0:scB:0:1", "b0:scA:1:1", "b0:scB:1:1",
                              "b0:scA:2:1", "b0:scB:2:1")
                self._attn_unit(tag="a4", s=s, h=h, u=u, qsrc=self.qt_scr, qtl=qtl,
                                kth=kth, vh=vh, praw=praw, pT=pT, bst=bst, dsb=dsb,
                                ot_sb=ot_sb, dQT=dQT, tM=tM, vV=vV, sS=sS,
                                expand_kv=False, ksrc=self.kt_scr, vsrc=self.v_scr, dKV=dKV,
                                prev_phase_done=("a5:x1ev:0:1:%d" % (NTS - 1)) if s == 1 else "b0:done_v",
                                qs_evts=None, phase_start_evts=pstart)

            for half in range(2):
                hf0 = half * 1024
                with nc.Block(no_gpsimd_drain=True) as b:
                    @b.sync
                    def _(e):
                        for j in range(16):
                            jj = (s * 2 + half) * 16 + j
                            if jj >= 2:
                                pj = jj - 2
                                self.wait(e, f"a5:pwcv:{pj // 32}:{(pj % 32) // 16}:{pj % 16}")
                            dPW[j % 2].dma(e, wstg[j % 2][:, 0:1024],
                                           self.wot[j * 128:(j + 1) * 128, hf0:hf0 + 1024])
                    @b.vector
                    def _(e):
                        for j in range(16):
                            jj = (s * 2 + half) * 16 + j
                            if jj == 0:
                                self.wait(e, "b0:done_t")
                            e.wait_ge(dPW[j % 2].h, 16 * (jj // 2 + 1))
                            vV.inc(e.tensor_copy(pw_b[:, j * 1024:(j + 1) * 1024], wstg[j % 2][:, 0:1024]))
                            self.evt(f"a5:pwcv:{s}:{half}:{j}", vV)
                for qi in range(NTS):
                    i = s * NTS + qi
                    t0 = qi * 128
                    with nc.Block(no_gpsimd_drain=True) as b:
                        @b.gpsimd
                        def _(e):
                            if (half, qi) != (0, 0):
                                ph, pq = (half, qi - 1) if qi > 0 else (0, NTS - 1)
                                self.wait(e, f"a5:x1ev:{s}:{ph}:{pq}")
                            dXR.dma(e, xr_f[:, hf0:hf0 + 1024],
                                    self.x2_scr[i * 128:(i + 1) * 128, hf0:hf0 + 1024])
                        @b.tensor
                        def _(e):
                            if qi == 0:
                                self.wait(e, f"a5:pwcv:{s}:{half}:15")
                                if half == 0:
                                    self.wait(e, f"a4:otdone:{s}")
                            if (half, qi) != (0, 0):
                                ph, pq = (half, qi - 1) if qi > 0 else (0, NTS - 1)
                                self.wait(e, f"a5:x1ev:{s}:{ph}:{pq}")
                            elif s == 1:
                                self.wait(e, "a5:x1ev:0:1:%d" % (NTS - 1))
                            for nq in range(2):
                                for hh in range(H):
                                    e.matmul(PA[:, nq * 512:(nq + 1) * 512],
                                             ot_sb[:, hh * T + t0:hh * T + t0 + 128],
                                             pw_b[:, hh * 1024 + nq * 512:hh * 1024 + (nq + 1) * 512],
                                             start=(hh == 0), stop=False)
                                inst = e.matmul(PA[:, nq * 512:(nq + 1) * 512], self.ones_b[0:1, :],
                                                bias_b[0:1, 3 * D + hf0 + nq * 512:3 * D + hf0 + (nq + 1) * 512],
                                                start=False, stop=True)
                            tM.inc(inst)
                            self.evt(f"a5:proj:{s}:{half}:{qi}", tM)
                        @b.vector
                        def _(e):
                            self.wait(e, f"a5:proj:{s}:{half}:{qi}")
                            e.wait_ge(dXR.h, 16 * (2 * s * NTS + half * NTS + qi + 1))
                            kk2 = 2 * s * NTS + half * NTS + qi
                            if kk2 >= 1:
                                e.wait_ge(dX1.h, 16 * kk2)
                            vV.inc(e.tensor_tensor(x1_f[:, hf0:hf0 + 1024], PA[:, 0:1024],
                                                   xr_f[:, hf0:hf0 + 1024], op=ALU.add))
                            self.evt(f"a5:x1ev:{s}:{half}:{qi}", vV)
                        @b.gpsimd
                        def _(e):
                            self.wait(e, f"a5:x1ev:{s}:{half}:{qi}")
                            dX1.dma(e, self.x3_scr[i * 128:(i + 1) * 128, hf0:hf0 + 1024],
                                    x1_f[:, hf0:hf0 + 1024])
        self.evt("a5:done", dX1)
        cP1.close()
        cB.close()

    def p_finish(self):
        nc = self.nc
        with nc.Block(no_gpsimd_drain=True) as b:
            @b.gpsimd
            def _(e):
                for sem in self.out_sems:
                    e.wait_ge(sem.h, sem.n)


def build(T=2048):
    p0 = Prog(T)
    p0.emit_all()
    p1 = Prog(T, table=p0.table)
    nc = p1.emit_all()
    return nc, p1


def host_inputs(inp, T=2048):
    C = np.ascontiguousarray
    m = {}
    m["wq"] = C(inp["L0_attn_a_wq"])
    m["wk_lat"] = C(inp["L0_attn_a_wk_lat"])
    m["wv_lat"] = C(inp["L0_attn_a_wv_lat"])
    m["kexp"] = C(inp["L0_attn_a_k_expand"])
    m["vexp"] = C(inp["L0_attn_a_v_expand"])
    m["projw"] = C(inp["L0_attn_a_proj_w"])
    m["wqt"] = C(inp["L0_attn_b_Wq"].T)
    m["wkt"] = C(inp["L0_attn_b_Wk"].T)
    m["wvt"] = C(inp["L0_attn_b_Wv"].T)
    m["wot"] = C(inp["L0_attn_b_Wo"].T)
    m["bq"] = C(inp["L0_attn_b_bq"])
    m["bk"] = C(inp["L0_attn_b_bk"])
    m["bv"] = C(inp["L0_attn_b_bv"])
    m["bo"] = C(inp["L0_attn_b_bo"])
    m["wgt"] = C(inp["L0_ffn_Wg"].T)
    m["w1t"] = C(inp["L0_ffn_W1"].transpose(0, 2, 1))
    m["w3t"] = C(inp["L0_ffn_W3"].transpose(0, 2, 1))
    m["w2t"] = C(inp["L0_ffn_W2"].transpose(0, 2, 1))
    m["ws1t"] = C(inp["L0_ffn_Ws1"].T)
    m["ws3t"] = C(inp["L0_ffn_Ws3"].T)
    m["ws2t"] = C(inp["L0_ffn_Ws2"].T)
    m["eye"] = np.eye(128).astype(bfdt)
    tri = np.zeros((128, 128), np.float32)
    tri[np.triu_indices(128, 1)] = BIG_NEG
    m["tri"] = tri
    trit = np.zeros((128, 4 * 512), np.float32)
    for j in range(4):
        r = np.arange(128)[:, None]
        c = np.arange(512)[None, :]
        trit[:, j * 512:(j + 1) * 512] = np.where(c >= j * 128 + r, 0.0, BIG_NEG)
    m["trit"] = trit.astype(bfdt)
    cq, sq = rope_tables(T, DH, scale=float(1.0 / np.sqrt(DH)))
    m["cqr"] = C(np.tile(cq, (1, H)))
    m["sqr"] = C(np.tile(sq, (1, H)))
    ck, sk = rope_tables(T, DL)
    m["ckr"] = C(ck)
    m["skr"] = C(sk)
    return m


def kernel(**inputs):
    T = 2048
    nc, prog = build(T=T)
    shared = host_inputs(inputs, T=T)
    x = np.ascontiguousarray(inputs["L0_attn_a_x"]).reshape(16, T, D)
    in_maps = []
    for c in range(NCORES):
        m = dict(shared)
        m["x_in"] = np.ascontiguousarray(x[2 * c:2 * c + 2].reshape(2 * T, D))
        in_maps.append(m)
    res = run_bass_kernel_spmd(nc, in_maps, core_ids=list(range(NCORES)))
    out = np.concatenate([r["out"].reshape(2, T, D) for r in res.results], axis=0)
    return out
